# version 01
# baseline (speedup 1.0000x reference)
import math
import contextlib
import numpy as np

import concourse.bass as bass
import concourse.mybir as mybir
from concourse.bass_utils import run_bass_kernel_spmd

F32 = mybir.dt.float32
AF = mybir.ActivationFunctionType
ALU = mybir.AluOpType
AX = mybir.AxisListType

B, S, D = 16, 2048, 1024
NCORES = 8
NSEQ = B // NCORES
DH = 64
FH, FE, FP = 8, 8, 128
SCALE_A = 1.0 / math.sqrt(DH)
SCALE_F = 1.0 / math.sqrt(FP)
EPS = 1e-5
NEG = -1.0e30
TT = S // 128
NQG = S // 512
ENGS = ("sync", "vector", "scalar", "tensor", "gpsimd")


def host_consts():
    ident = np.eye(128, dtype=np.float32)
    j = np.arange(32, dtype=np.float64)
    invf = 1.0 / (10000.0 ** (j / 32.0))
    t = np.arange(S, dtype=np.float64)
    ang = invf[:, None] * t[None, :]
    cos_rep = np.tile(np.cos(ang), (4, 1))
    sin_rep = np.tile(np.sin(ang), (4, 1))
    sgn = np.ones((128, 1))
    sgn[0:32] = -1.0
    sgn[64:96] = -1.0
    sin_rep = sin_rep * sgn
    cos_hi = cos_rep.astype(np.float32)
    cos_lo = (cos_rep - cos_hi.astype(np.float64)).astype(np.float32)
    sin_hi = sin_rep.astype(np.float32)
    sin_lo = (sin_rep - sin_hi.astype(np.float64)).astype(np.float32)
    kp = np.arange(128)[:, None]
    qf = np.arange(512)[None, :]
    masks = np.zeros((4, 128, 512), dtype=np.float32)
    for o in range(4):
        masks[o] = np.where(kp <= qf - o * 128, 0.0, NEG)
    return {
        "c_ident": ident,
        "c_cos_hi": cos_hi, "c_cos_lo": cos_lo,
        "c_sin_hi": sin_hi, "c_sin_lo": sin_lo,
        "c_masks": masks,
    }


eT, eV, eA, eP = "eT", "eV", "eA", "eP"
dC, dX, dW, dF, dG, dO, dR = "dC", "dX", "dW", "dF", "dG", "dO", "dR"
ALL_SEMS = [eT, eV, eA, eP, dC, dX, dW, dF, dG, dO, dR]
OWN = {"tensor": eT, "vector": eV, "scalar": eA, "gpsimd": eP}


class Prog:

    def __init__(self):
        self.items = {e: [] for e in ENGS}
        self.cnt = {sm: 0 for sm in ALL_SEMS}

    def op(self, eng, fn, incs=()):
        incs = tuple(incs)
        for sm, a in incs:
            self.cnt[sm] += a
        self.items[eng].append(("op", fn, incs))
        return tuple((sm, self.cnt[sm]) for sm, _ in incs)

    def wait(self, eng, sem, v=None):
        if v is None:
            v = self.cnt[sem]
        if v > 0:
            self.items[eng].append(("wait", sem, v))

    def tok(self, sem):
        return (sem, self.cnt[sem])

    def wait_tok(self, eng, *toks):
        for sem, v in toks:
            if v > 0:
                self.items[eng].append(("wait", sem, v))


def build(taps=(), upto="final"):
    taps = set(taps)
    nc = bass.Bass("TRN2", target_bir_lowering=False, detect_race_conditions=False)

    x_d = nc.dram_tensor("x", [NSEQ, S, D], F32, kind="ExternalInput")
    wqkv_d = nc.dram_tensor("w_qkv", [D, 3 * D], F32, kind="ExternalInput")
    wout_d = nc.dram_tensor("w_out", [D, D], F32, kind="ExternalInput")
    qw_d = nc.dram_tensor("q_w", [FH * FP, D], F32, kind="ExternalInput")
    kw_d = nc.dram_tensor("k_w", [FH * FP, D], F32, kind="ExternalInput")
    vw_d = nc.dram_tensor("v_w", [FH * FE, D, FP], F32, kind="ExternalInput")
    ow_d = nc.dram_tensor("o_w", [FH * FE, FP, D], F32, kind="ExternalInput")
    selv_d = nc.dram_tensor("sel_v", [FH * FE, D], F32, kind="ExternalInput")
    selo_d = nc.dram_tensor("sel_o", [FH * FE, D], F32, kind="ExternalInput")
    rs_d = nc.dram_tensor("route_scale", [1], F32, kind="ExternalInput")
    ident_d = nc.dram_tensor("c_ident", [128, 128], F32, kind="ExternalInput")
    cos_hi_d = nc.dram_tensor("c_cos_hi", [128, S], F32, kind="ExternalInput")
    cos_lo_d = nc.dram_tensor("c_cos_lo", [128, S], F32, kind="ExternalInput")
    sin_hi_d = nc.dram_tensor("c_sin_hi", [128, S], F32, kind="ExternalInput")
    sin_lo_d = nc.dram_tensor("c_sin_lo", [128, S], F32, kind="ExternalInput")
    masks_d = nc.dram_tensor("c_masks", [4, 128, 512], F32, kind="ExternalInput")
    out_d = nc.dram_tensor("out", [NSEQ, S, D], F32, kind="ExternalOutput")
    oscr_d = nc.dram_tensor("oscr", [8, 128, S], F32, kind="Internal")

    tap_d = {}
    tap_shapes = {
        "xn": [128, 8 * S], "gv": [128, TT * 64], "go": [128, TT * 64],
        "qrot": [128, S], "krot": [128, S], "vaug": [128, TT * 130],
        "ctxa": [128, TT * 64], "scrA": [8, 128, S], "vacc": [128, TT * 130],
        "ctxf": [128, TT * 128], "scrF": [8, 128, S],
        "qf": [128, S], "kf": [128, S],
    }
    for t in taps:
        tap_d[t] = nc.dram_tensor("tap_" + t, tap_shapes[t], F32, kind="ExternalOutput")

    A = nc.alloc_sbuf_tensor
    xn = A("xn", [128, 8, S], F32)
    cos_hi = A("cos_hi", [128, S], F32)
    sin_hi = A("sin_hi", [128, S], F32)
    gates_v = A("gates_v", [128, TT, 64], F32)
    gates_o = A("gates_o", [128, TT, 64], F32)
    selT = A("selT", [128, 2, 8, 64], F32)
    id_sb = A("id_sb", [128, 128], F32)
    iem9 = A("iem9", [128, 64], F32)
    rs_col = A("rs_col", [128, 1], F32)
    eps_col = A("eps_col", [128, 1], F32)
    ones_row = A("ones_row", [1, 128], F32)
    rs_sb = A("rs_sb", [1, 1], F32)
    sc = A("sc", [128, 32], F32)
    gt = A("gt", [128, 640], F32)
    big1 = A("big1", [128, S], F32)
    big2 = A("big2", [128, S], F32)
    big3 = A("big3", [128, S], F32)
    big4 = A("big4", [128, TT, 130], F32)
    big5 = A("big5", [128, 6144], F32)
    big6 = A("big6", [128, 4096], F32)
    big7 = A("big7", [128, 2048], F32)
    big8 = A("big8", [128, S], F32)
    ctxn = A("ctxn", [128, TT, 64], F32)
    pT = A("pT", [128, 2, 512], F32)
    evac = A("evac", [128, 2, 512], F32)
    sbuf_left = nc.sbuf_bytes_remaining

    wq_sl = big5.ap()[:, 0:3072].rearrange("p (c j u) -> p c j u", c=8, j=3)
    sin_lo = big5.ap()[:, 3072:5120]
    cos_lo = big6.ap()[:, 2048:4096]
    tmpg = pT.ap().rearrange("p b u -> p b u")[:, :, 0:128]
    vw_sl = big5.ap()[:, 0:4096].rearrange("p (b c e u) -> p b c e u", b=2, c=8, e=2)
    ow_sl = big5.ap()[:, 4096:6144].rearrange("p (b e u) -> p b e u", b=2, e=8)
    ln_x = big5.ap()[:, 0:2048].rearrange("p (b u) -> p b u", b=2)
    ln_s1 = big5.ap()[:, 2048:4096].rearrange("p (b u) -> p b u", b=2)
    ln_s2 = big5.ap()[:, 4096:6144].rearrange("p (b u) -> p b u", b=2)
    fin_in = big5.ap()[:, 0:2048].rearrange("p (b c u) -> p b c u", b=2, c=8)
    fin_x = big5.ap()[:, 2048:4096].rearrange("p (b u) -> p b u", b=2)
    fin_o = big5.ap()[:, 4096:6144].rearrange("p (b u) -> p b u", b=2)
    ctx_pair = big6.ap()[:, 0:2048]
    w_he = big6.ap().rearrange("p (e u) -> p e u", e=8)
    masks_v = big7.ap().rearrange("p (o u) -> p o u", o=4)
    qwT = big7.ap()[:, 0:1024].rearrange("p (c u) -> p c u", c=8)
    kwT = big7.ap()[:, 1024:2048].rearrange("p (c u) -> p c u", c=8)
    ctxn_f = big3.ap().rearrange("p (t u) -> p t u", t=TT)
    shuf = big3.ap()
    ctxB = big3.ap()
    raw = big8.ap()
    wout_v = big8.ap()[:, 0:1024].rearrange("p (c u) -> p c u", c=8)
    sel_raw64 = big8.ap()[0:64, 0:1024]

    ps_a = nc.alloc_psum_tensor("ps_a", [128, 512], F32)
    ps_b = nc.alloc_psum_tensor("ps_b", [128, 512], F32)
    ps_big = nc.alloc_psum_tensor("ps_big", [128, 2, 512], F32)
    ps_ctx = nc.alloc_psum_tensor("ps_ctx", [128, 4, 512], F32)
    ps_bigf = ps_big.ap().rearrange("p a u -> p (a u)")

    p = Prog()

    def T(fn, inc=1):
        return p.op("tensor", fn, ((eT, inc),) if inc else ())

    def V(fn, inc=1):
        return p.op("vector", fn, ((eV, inc),) if inc else ())

    def Ac(fn, inc=1):
        return p.op("scalar", fn, ((eA, inc),) if inc else ())

    def G(fn, inc=1):
        return p.op("gpsimd", fn, ((eP, inc),) if inc else ())

    def dma(queue_eng, sem, out, in_, accum=False):
        def fn(eng, out=out, in_=in_, accum=accum):
            if accum:
                return eng.dma_start(out=out, in_=in_, accum_op=ALU.add)
            return eng.dma_start(out=out, in_=in_)
        return p.op(queue_eng, fn, ((sem, 16),))

    def join():
        for e in ENGS:
            own = OWN.get(e)
            for sm in ALL_SEMS:
                if sm != own:
                    p.wait(e, sm)

    def tap(name, src_ap):
        if name in tap_d:
            dma("sync", dO, tap_d[name].ap(), src_ap)

    dma("sync", dC, id_sb[:], ident_d[:])
    dma("sync", dC, cos_hi[:], cos_hi_d[:])
    dma("sync", dC, sin_hi[:], sin_hi_d[:])
    dma("sync", dC, rs_sb[:], rs_d.ap().rearrange("(a b) -> a b", a=1))

    G(lambda g: g.iota(out=iem9[:], pattern=[[0, 8], [1, 8]], base=-9,
                       channel_multiplier=0, allow_small_or_imprecise_dtypes=True))
    G(lambda g: g.memset(eps_col[:], EPS))
    G(lambda g: g.memset(ones_row[:], 1.0))

    p.wait("tensor", dC)
    p.wait("tensor", eP)
    T(lambda t: t.matmul(out=ps_a[:, 0:1], lhsT=ones_row[:], rhs=rs_sb[:],
                         start=True, stop=True))
    p.wait("vector", eT)
    V(lambda v: v.tensor_copy(out=rs_col[:], in_=ps_a[:, 0:1]))

    for si, sd in enumerate((selv_d, selo_d)):
        p.wait("sync", eT)
        tk = dma("sync", dC, sel_raw64, sd[:])
        p.wait_tok("tensor", *tk)
        p.wait("tensor", eV)
        for c in range(8):
            ps = ps_a if c % 2 == 0 else ps_b
            if si > 0 or c >= 2:
                p.wait("tensor", eV)
            T(lambda t, c=c, ps=ps: t.transpose(
                out=ps[0:128, 0:64], in_=sel_raw64[:, c * 128:(c + 1) * 128],
                identity=id_sb[0:64, 0:64]))
            p.wait("vector", eT)
            V(lambda v, si=si, c=c, ps=ps: v.tensor_copy(
                out=selT[:, si, c, :], in_=ps[0:128, 0:64]))
        p.wait("tensor", eV)

    def st_ln(s):
        join()
        dma("sync", dC, masks_v, masks_d.ap().rearrange("o p u -> p o u"))
        for tt in range(TT):
            par = tt % 2
            o = par * 8
            p.wait("sync", eV)
            xt = ln_x[:, par, :]
            tkx = dma("sync", dX, xt, x_d[s, tt * 128:(tt + 1) * 128, :])
            p.wait_tok("vector", *tkx)
            p.wait("vector", eT)
            V(lambda v, xt=xt, o=o: v.reduce_sum(out=sc[:, o:o + 1], in_=xt, axis=AX.X))
            p.wait("vector", eV)
            V(lambda v, o=o: v.tensor_scalar(out=sc[:, o + 1:o + 2], in0=sc[:, o:o + 1],
                                             scalar1=1.0 / D, scalar2=None, op0=ALU.mult))
            p.wait("vector", eV)
            dev = ln_s1[:, par, :]
            tkd = V(lambda v, xt=xt, dev=dev, o=o: v.tensor_scalar(
                out=dev, in0=xt, scalar1=sc[:, o + 1:o + 2], scalar2=None,
                op0=ALU.subtract))
            p.wait_tok("scalar", *tkd)
            Ac(lambda a, dev=dev, par=par, o=o: a.activation(
                out=ln_s2[:, par, :], in_=dev, func=AF.Square,
                accum_out=sc[:, o + 2:o + 3]))
            p.wait("scalar", eA)
            tka = Ac(lambda a, o=o: a.activation(
                out=sc[:, o + 5:o + 6], in_=sc[:, o + 2:o + 3], func=AF.Sqrt,
                scale=1.0 / D, bias=eps_col[:]))
            p.wait_tok("vector", *tka)
            V(lambda v, o=o: v.reciprocal(out=sc[:, o + 3:o + 4], in_=sc[:, o + 5:o + 6]))
            p.wait("vector", eV)
            V(lambda v, o=o: v.tensor_tensor(out=sc[:, o + 4:o + 5],
                                             in0=sc[:, o + 1:o + 2],
                                             in1=sc[:, o + 3:o + 4], op=ALU.mult))
            p.wait("vector", eV)
            tkn = V(lambda v, xt=xt, dev=dev, o=o: v.tensor_scalar(
                out=dev, in0=xt, scalar1=sc[:, o + 3:o + 4],
                scalar2=sc[:, o + 4:o + 5], op0=ALU.mult, op1=ALU.subtract))
            p.wait_tok("tensor", *tkn)
            for c in range(8):
                ps = ps_a if c % 2 == 0 else ps_b
                if tt > 0 or c >= 2:
                    p.wait("tensor", eV)
                T(lambda t, dev=dev, c=c, ps=ps: t.transpose(
                    out=ps[:, 0:128], in_=dev[:, c * 128:(c + 1) * 128],
                    identity=id_sb[:]))
                p.wait("vector", eT)
                V(lambda v, c=c, tt=tt, ps=ps: v.tensor_copy(
                    out=xn[:, c, tt * 128:(tt + 1) * 128], in_=ps[:, 0:128]))
        if s == 0 and "xn" in tap_d:
            join()
            tap("xn", xn.ap().rearrange("p c t -> p (c t)"))

    def st_gates(s):
        join()

        def b8(ap):
            return ap.rearrange("p (g e) -> p g e", e=1).broadcast_to([128, 8, 8])

        IE3 = iem9.ap().rearrange("p (g e) -> p g e", e=8)
        for tt in range(TT):
            if tt > 0:
                p.wait("tensor", eA)
            for si in range(2):
                psg = ps_a if si == 0 else ps_b
                for c in range(8):
                    T(lambda t, si=si, c=c, tt=tt, psg=psg: t.matmul(
                        out=psg[:, 0:64],
                        lhsT=xn[:, c, tt * 128:(tt + 1) * 128],
                        rhs=selT[:, si, c, :],
                        start=(c == 0), stop=(c == 7)), inc=(1 if c == 7 else 0))
            p.wait("scalar", eT)
            p.wait("scalar", eV)
            Ac(lambda a: a.activation(out=gt[:, 0:64], in_=ps_a[:, 0:64],
                                      func=AF.Sigmoid), inc=0)
            Ac(lambda a: a.activation(out=gt[:, 64:128], in_=ps_b[:, 0:64],
                                      func=AF.Sigmoid))
            p.wait("vector", eA)
            for si, gout in ((0, gates_v), (1, gates_o)):
                Pa = gt.ap()[:, si * 64:(si + 1) * 64]
                P3v = Pa.rearrange("p (g e) -> p g e", e=8)
                M1 = gt.ap()[:, 192:200]
                I1 = gt.ap()[:, 200:208]
                M2 = gt.ap()[:, 208:216]
                I2 = gt.ap()[:, 216:224]
                SS = gt.ap()[:, 224:232]
                R = gt.ap()[:, 232:240]
                W1 = gt.ap()[:, 240:248]
                W2 = gt.ap()[:, 248:256]
                E1 = gt.ap()[:, 256:320].rearrange("p (g e) -> p g e", e=8)
                OH1 = gt.ap()[:, 320:384].rearrange("p (g e) -> p g e", e=8)
                PM = gt.ap()[:, 384:448]
                PM3 = PM.rearrange("p (g e) -> p g e", e=8)
                E2 = gt.ap()[:, 448:512].rearrange("p (g e) -> p g e", e=8)
                OH2 = gt.ap()[:, 512:576].rearrange("p (g e) -> p g e", e=8)
                CT = gt.ap()[:, 576:640].rearrange("p (g e) -> p g e", e=8)
                CTf = gt.ap()[:, 576:640]

                p.wait("vector", eV)
                V(lambda v, P3v=P3v, M1=M1: v.reduce_max(out=M1, in_=P3v, axis=AX.X), inc=1)
                p.wait("vector", eV)
                V(lambda v, P3v=P3v, M1=M1, E1=E1: v.tensor_tensor(
                    out=E1, in0=P3v, in1=b8(M1), op=ALU.is_equal), inc=1)
                p.wait("vector", eV)
                V(lambda v, E1=E1, CT=CT: v.tensor_tensor(
                    out=CT, in0=E1, in1=IE3, op=ALU.mult), inc=1)
                p.wait("vector", eV)
                V(lambda v, CT=CT, I1=I1: v.tensor_reduce(
                    out=I1, in_=CT, axis=AX.X, op=ALU.min), inc=1)
                p.wait("vector", eV)
                V(lambda v, I1=I1, OH1=OH1: v.tensor_tensor(
                    out=OH1, in0=IE3, in1=b8(I1), op=ALU.is_equal), inc=1)
                p.wait("vector", eV)
                V(lambda v, OH1=OH1, Pa=Pa, PM=PM: v.scalar_tensor_tensor(
                    out=PM, in0=OH1.rearrange("p g e -> p (g e)"), scalar=-2.0,
                    in1=Pa, op0=ALU.mult, op1=ALU.add), inc=1)
                p.wait("vector", eV)
                V(lambda v, PM3=PM3, M2=M2: v.reduce_max(out=M2, in_=PM3, axis=AX.X), inc=1)
                p.wait("vector", eV)
                V(lambda v, PM3=PM3, M2=M2, E2=E2: v.tensor_tensor(
                    out=E2, in0=PM3, in1=b8(M2), op=ALU.is_equal), inc=1)
                p.wait("vector", eV)
                V(lambda v, E2=E2, CT=CT: v.tensor_tensor(
                    out=CT, in0=E2, in1=IE3, op=ALU.mult), inc=1)
                p.wait("vector", eV)
                V(lambda v, CT=CT, I2=I2: v.tensor_reduce(
                    out=I2, in_=CT, axis=AX.X, op=ALU.min), inc=1)
                p.wait("vector", eV)
                V(lambda v, I2=I2, OH2=OH2: v.tensor_tensor(
                    out=OH2, in0=IE3, in1=b8(I2), op=ALU.is_equal), inc=1)
                p.wait("vector", eV)
                V(lambda v, M1=M1, M2=M2, SS=SS: v.tensor_tensor(
                    out=SS, in0=M1, in1=M2, op=ALU.add), inc=1)
                p.wait("vector", eV)
                V(lambda v, SS=SS: v.tensor_scalar(out=SS, in0=SS, scalar1=1e-9,
                                                   scalar2=None, op0=ALU.max), inc=1)
                p.wait("vector", eV)
                V(lambda v, SS=SS, R=R: v.reciprocal(out=R, in_=SS), inc=1)
                p.wait("vector", eV)
                V(lambda v, M1=M1, R=R, W1=W1: v.tensor_tensor(
                    out=W1, in0=M1, in1=R, op=ALU.mult), inc=1)
                p.wait("vector", eV)
                V(lambda v, W1=W1: v.tensor_scalar(out=W1, in0=W1, scalar1=rs_col[:],
                                                   scalar2=None, op0=ALU.mult), inc=1)
                p.wait("vector", eV)
                V(lambda v, M2=M2, R=R, W2=W2: v.tensor_tensor(
                    out=W2, in0=M2, in1=R, op=ALU.mult), inc=1)
                p.wait("vector", eV)
                V(lambda v, W2=W2: v.tensor_scalar(out=W2, in0=W2, scalar1=rs_col[:],
                                                   scalar2=None, op0=ALU.mult), inc=1)
                p.wait("vector", eV)
                V(lambda v, OH1=OH1, W1=W1, CT=CT: v.tensor_tensor(
                    out=CT, in0=OH1, in1=b8(W1), op=ALU.mult), inc=1)
                p.wait("vector", eV)
                V(lambda v, OH2=OH2, W2=W2, PM3=PM3: v.tensor_tensor(
                    out=PM3, in0=OH2, in1=b8(W2), op=ALU.mult), inc=1)
                p.wait("vector", eV)
                V(lambda v, CTf=CTf, PM=PM, gout=gout, tt=tt: v.tensor_tensor(
                    out=gout[:, tt, :], in0=CTf, in1=PM, op=ALU.add), inc=1)
        if s == 0 and ("gv" in tap_d or "go" in tap_d):
            join()
            tap("gv", gates_v.ap().rearrange("p t u -> p (t u)"))
            tap("go", gates_o.ap().rearrange("p t u -> p (t u)"))

    def rope_one(dst):
        V(lambda v, dst=dst: v.tensor_tensor(out=dst, in0=raw, in1=cos_hi[:], op=ALU.mult), inc=0)
        V(lambda v: v.tensor_tensor(out=raw, in0=raw, in1=cos_lo, op=ALU.mult), inc=0)
        V(lambda v, dst=dst: v.tensor_tensor(out=dst, in0=dst, in1=raw, op=ALU.add), inc=0)
        V(lambda v: v.tensor_tensor(out=raw, in0=shuf, in1=sin_hi[:], op=ALU.mult), inc=0)
        V(lambda v, dst=dst: v.tensor_tensor(out=dst, in0=dst, in1=raw, op=ALU.add), inc=0)
        V(lambda v: v.tensor_tensor(out=raw, in0=shuf, in1=sin_lo, op=ALU.mult), inc=0)
        V(lambda v, dst=dst: v.tensor_tensor(out=dst, in0=dst, in1=raw, op=ALU.add), inc=1)

    def attn_core(lhsK, rhsQ, vA, cW, causal, ctx_out_fn):
        expTok = [None, None]
        ctxTok = [None, None]
        p.wait("tensor", eA)
        p.wait("scalar", eT)
        for qg in range(NQG):
            nkt = (4 * qg + 4) if causal else TT
            p.wait("tensor", eV)
            for kt in range(nkt):
                pr = kt % 2
                psS = ps_big.ap()[:, pr, :]
                if expTok[pr] is not None:
                    p.wait_tok("tensor", expTok[pr])
                tb = T(lambda t, kt=kt, qg=qg, psS=psS: t.matmul(
                    out=psS, lhsT=lhsK(kt), rhs=rhsQ(qg), start=True, stop=True))
                lastS = tb
                o = kt - 4 * qg
                if causal and o >= 0:
                    p.wait_tok("vector", *tb)
                    lastS = V(lambda v, psS=psS, o=o: v.tensor_tensor(
                        out=psS, in0=psS, in1=masks_v[:, o, :], op=ALU.add))
                    p.wait_tok("scalar", *lastS)
                else:
                    p.wait_tok("scalar", *lastS)
                if ctxTok[pr] is not None:
                    p.wait_tok("scalar", ctxTok[pr])
                ta = Ac(lambda a, psS=psS, pr=pr: a.activation(
                    out=pT[:, pr, :], in_=psS, func=AF.Exp))
                expTok[pr] = ta[0]
                p.wait_tok("tensor", *ta)
                for qtl in range(4):
                    T(lambda t, pr=pr, qtl=qtl, kt=kt, nkt=nkt: t.matmul(
                        out=ps_ctx[:, qtl, cW[0]:cW[0] + cW[1]],
                        lhsT=pT[:, pr, qtl * 128:(qtl + 1) * 128],
                        rhs=vA(kt),
                        start=(kt == 0), stop=(kt == nkt - 1)),
                      inc=(1 if (qtl == 3 and kt == nkt - 1) else 0))
                ctxTok[pr] = p.tok(eT)
            ctx_out_fn(qg)

    first_in_seq = [True]

    def st_attn(s):
        join()
        G(lambda g: g.memset(big4.ap()[:, :, 64:65], 1.0))
        G(lambda g: g.memset(big4.ap()[:, :, 129:130], 1.0))
        p.wait("tensor", eP)
        dma("sync", dW, cos_lo, cos_lo_d[:])
        dma("sync", dW, sin_lo, sin_lo_d[:])
        for hp in range(8):
            p.wait("sync", eT)
            tkw = None
            for j, c0 in enumerate((hp * 128, D + hp * 128, 2 * D + hp * 128)):
                tkw = dma("sync", dW, wq_sl[:, :, j, :],
                          wqkv_d.ap().rearrange("(c p) n -> p c n", p=128)[:, :, c0:c0 + 128])
            p.wait_tok("tensor", *tkw)
            for j in (0, 1):
                scale = SCALE_A if j == 0 else 1.0
                p.wait("scalar", eV)
                p.wait("scalar", eT)
                for nt in range(NQG):
                    ps = ps_a if nt % 2 == 0 else ps_b
                    p.wait("tensor", eA)
                    for c in range(8):
                        T(lambda t, j=j, c=c, nt=nt, ps=ps: t.matmul(
                            out=ps[:, :], lhsT=wq_sl[:, c, j, :],
                            rhs=xn[:, c, nt * 512:(nt + 1) * 512],
                            start=(c == 0), stop=(c == 7)), inc=(1 if c == 7 else 0))
                    p.wait("scalar", eT)
                    Ac(lambda a, nt=nt, scale=scale, ps=ps: a.mul(
                        out=raw[:, nt * 512:(nt + 1) * 512], in_=ps[:, :], mul=scale))
                p.wait("gpsimd", eA)
                p.wait("gpsimd", eV)
                for (d0, s0) in ((0, 32), (32, 0), (64, 96), (96, 64)):
                    dma("gpsimd", dG, big3.ap()[d0:d0 + 32, :], raw[s0:s0 + 32, :])
                p.wait("vector", dG)
                p.wait("vector", dW)
                p.wait("vector", eA)
                p.wait("vector", eT)
                rope_one((big1 if j == 0 else big2).ap())
            for tt in range(TT):
                ps = ps_a if tt % 2 == 0 else ps_b
                p.wait("tensor", eV)
                for c in range(8):
                    T(lambda t, c=c, tt=tt, ps=ps: t.matmul(
                        out=ps[:, 0:128], lhsT=xn[:, c, tt * 128:(tt + 1) * 128],
                        rhs=wq_sl[:, c, 2, :],
                        start=(c == 0), stop=(c == 7)), inc=(1 if c == 7 else 0))
                p.wait("vector", eT)
                V(lambda v, tt=tt, ps=ps: v.tensor_copy(
                    out=big4.ap()[:, tt, :].rearrange("p (b u) -> p b u", b=2)[:, :, 0:64],
                    in_=ps[:, 0:128].rearrange("p (b u) -> p b u", b=2)))
            if s == 0 and hp == 0:
                join()
                tap("qrot", big1.ap())
                tap("krot", big2.ap())
                tap("vaug", big4.ap().rearrange("p t u -> p (t u)"))
            for i in range(2):
                R0 = slice(64 * i, 64 * i + 64)
                cA = 65 * i

                def ctx_out(qg, cA=cA, i=i):
                    p.wait("vector", eT)
                    for qtl in range(4):
                        V(lambda v, qtl=qtl, cA=cA: v.reciprocal(
                            out=sc[:, 16 + qtl:17 + qtl],
                            in_=ps_ctx[:, qtl, cA + 64:cA + 65]),
                          inc=(1 if qtl == 3 else 0))
                    p.wait("vector", eV)
                    for qtl in range(4):
                        qt = qg * 4 + qtl
                        V(lambda v, qtl=qtl, qt=qt, cA=cA: v.tensor_scalar(
                            out=ctxn[:, qt, :], in0=ps_ctx[:, qtl, cA:cA + 64],
                            scalar1=sc[:, 16 + qtl:17 + qtl], scalar2=None,
                            op0=ALU.mult), inc=(1 if qtl == 3 else 0))

                attn_core(
                    lhsK=lambda kt, R0=R0: big2.ap()[R0, kt * 128:(kt + 1) * 128],
                    rhsQ=lambda qg, R0=R0: big1.ap()[R0, qg * 512:(qg + 1) * 512],
                    vA=lambda kt, cA=cA: big4.ap()[:, kt, cA:cA + 65],
                    cW=(cA, 65), causal=True, ctx_out_fn=ctx_out)
                p.wait("tensor", eV)
                for qt in range(TT):
                    ps = ps_a if qt % 2 == 0 else ps_b
                    p.wait("tensor", eV)
                    T(lambda t, qt=qt, ps=ps: t.transpose(
                        out=ps[0:64, 0:128], in_=ctxn[:, qt, :], identity=id_sb[:]))
                    p.wait("vector", eT)
                    dst = ctx_pair if i == 0 else ctxB
                    V(lambda v, qt=qt, ps=ps, dst=dst: v.tensor_copy(
                        out=dst[0:64, qt * 128:(qt + 1) * 128], in_=ps[0:64, 0:128]))
                if i == 1:
                    p.wait("gpsimd", eV)
                    dma("gpsimd", dG, ctx_pair[64:128, :], ctxB[0:64, :])
            if s == 0 and hp == 0:
                join()
                tap("ctxa", ctxn.ap().rearrange("p t u -> p (t u)"))
            p.wait("sync", eV)
            tkwo = dma("sync", dW, wout_v,
                       wout_d[hp * 128:(hp + 1) * 128, :].rearrange("p (c u) -> p c u", c=8))
            p.wait_tok("tensor", *tkwo)
            p.wait("tensor", dG)
            p.wait("tensor", eV)
            woTok = [None, None]
            for dc in range(8):
                for jj in range(4):
                    pr2 = (dc * 4 + jj) % 2
                    p.wait("tensor", eA)
                    T(lambda t, dc=dc, jj=jj: t.matmul(
                        out=ps_big[:, jj % 2, :], lhsT=wout_v[:, dc, :],
                        rhs=ctx_pair[:, jj * 512:(jj + 1) * 512],
                        start=True, stop=True))
                    p.wait("scalar", eT)
                    if woTok[pr2] is not None:
                        p.wait_tok("scalar", woTok[pr2])
                    tae = Ac(lambda a, pr2=pr2, jj=jj: a.copy(
                        out=evac[:, pr2, :], in_=ps_big[:, jj % 2, :]))
                    p.wait_tok("gpsimd", *tae)
                    tkg = dma("gpsimd", dG, oscr_d[dc, :, jj * 512:(jj + 1) * 512],
                              evac[:, pr2, :], accum=(hp != 0))
                    woTok[pr2] = tkg[0]
        if s == 0 and "scrA" in tap_d:
            join()
            tap("scrA", oscr_d.ap())

    def st_ffn(s):
        join()
        G(lambda g: g.memset(big4.ap()[:, :, 128:129], 1.0))
        p.wait("tensor", eP)
        p.wait("vector", eP)
        owTok = [None, None]
        for h in range(8):
            for wi, wd in enumerate((qw_d, kw_d)):
                p.wait("sync", eT)
                tk = dma("sync", dF, raw[:, 0:1024], wd[h * 128:(h + 1) * 128, :])
                p.wait_tok("tensor", *tk)
                p.wait("tensor", eA)
                for c in range(8):
                    ps = ps_a if c % 2 == 0 else ps_b
                    p.wait("tensor", eV)
                    T(lambda t, c=c, ps=ps: t.transpose(
                        out=ps[:, 0:128], in_=raw[:, c * 128:(c + 1) * 128],
                        identity=id_sb[:]))
                    p.wait("vector", eT)
                    dstv = qwT if wi == 0 else kwT
                    V(lambda v, c=c, ps=ps, dstv=dstv: v.tensor_copy(
                        out=dstv[:, c, :], in_=ps[:, 0:128]))
            for wi in (0, 1):
                dstb = big1 if wi == 0 else big2
                scale = SCALE_F if wi == 0 else 1.0
                srcv = qwT if wi == 0 else kwT
                for nt in range(NQG):
                    ps = ps_a if nt % 2 == 0 else ps_b
                    p.wait("tensor", eA)
                    p.wait("tensor", eV)
                    for c in range(8):
                        T(lambda t, c=c, nt=nt, srcv=srcv, ps=ps: t.matmul(
                            out=ps[:, :], lhsT=srcv[:, c, :],
                            rhs=xn[:, c, nt * 512:(nt + 1) * 512],
                            start=(c == 0), stop=(c == 7)), inc=(1 if c == 7 else 0))
                    p.wait("scalar", eT)
                    p.wait("scalar", eV)
                    Ac(lambda a, nt=nt, scale=scale, dstb=dstb, ps=ps: a.mul(
                        out=dstb.ap()[:, nt * 512:(nt + 1) * 512], in_=ps[:, :], mul=scale))
            for eg in range(4):
                par = eg % 2
                p.wait("sync", eT)
                for e2 in range(2):
                    tk = dma("sync", dF, vw_sl[:, par, :, e2, :],
                             vw_d[h * 8 + eg * 2 + e2, :, :]
                             .rearrange("(c p) u -> p c u", p=128))
                p.wait_tok("tensor", *tk)
                for tt in range(TT):
                    ps = ps_a if tt % 2 == 0 else ps_b
                    p.wait("tensor", eV)
                    for c in range(8):
                        T(lambda t, c=c, tt=tt, par=par, ps=ps: t.matmul(
                            out=ps[:, 0:256], lhsT=xn[:, c, tt * 128:(tt + 1) * 128],
                            rhs=vw_sl[:, par, c, :, :],
                            start=(c == 0), stop=(c == 7)), inc=(1 if c == 7 else 0))
                    p.wait("vector", eT)
                    for e2 in range(2):
                        he = h * 8 + eg * 2 + e2
                        if e2 == 1:
                            p.wait("vector", eV)
                        if eg == 0 and e2 == 0:
                            V(lambda v, tt=tt, e2=e2, he=he, ps=ps: v.tensor_scalar(
                                out=big4.ap()[:, tt, 0:128],
                                in0=ps[:, e2 * 128:(e2 + 1) * 128],
                                scalar1=gates_v[:, tt, he:he + 1], scalar2=None,
                                op0=ALU.mult), inc=1)
                        else:
                            V(lambda v, tt=tt, e2=e2, he=he, ps=ps: v.scalar_tensor_tensor(
                                out=big4.ap()[:, tt, 0:128],
                                in0=ps[:, e2 * 128:(e2 + 1) * 128],
                                scalar=gates_v[:, tt, he:he + 1],
                                in1=big4.ap()[:, tt, 0:128],
                                op0=ALU.mult, op1=ALU.add), inc=1)
            if s == 0 and h == 0:
                join()
                tap("qf", big1.ap())
                tap("kf", big2.ap())
                tap("vacc", big4.ap().rearrange("p t u -> p (t u)"))

            def ctx_out_f(qg):
                p.wait("vector", eT)
                for qtl in range(4):
                    V(lambda v, qtl=qtl: v.reciprocal(
                        out=sc[:, 24 + qtl:25 + qtl],
                        in_=ps_ctx[:, qtl, 128:129]),
                      inc=(1 if qtl == 3 else 0))
                p.wait("vector", eV)
                for qtl in range(4):
                    qt = qg * 4 + qtl
                    V(lambda v, qtl=qtl, qt=qt: v.tensor_scalar(
                        out=ctxn_f[:, qt, :], in0=ps_ctx[:, qtl, 0:128],
                        scalar1=sc[:, 24 + qtl:25 + qtl], scalar2=None,
                        op0=ALU.mult), inc=(1 if qtl == 3 else 0))

            attn_core(
                lhsK=lambda kt: big2.ap()[:, kt * 128:(kt + 1) * 128],
                rhsQ=lambda qg: big1.ap()[:, qg * 512:(qg + 1) * 512],
                vA=lambda kt: big4.ap()[:, kt, 0:129],
                cW=(0, 129), causal=False, ctx_out_fn=ctx_out_f)
            if s == 0 and h == 0:
                join()
                tap("ctxf", ctxn_f.rearrange("p t u -> p (t u)"))

            for tc in range(4):
                p.wait("vector", eA)
                for e in range(8):
                    he = h * 8 + e
                    for qtl in range(4):
                        qt = tc * 4 + qtl
                        pg = (e * 4 + qtl) % 2
                        p.wait("vector", eT)
                        V(lambda v, qt=qt, he=he, pg=pg: v.tensor_scalar(
                            out=tmpg[:, pg, :], in0=ctxn_f[:, qt, :],
                            scalar1=gates_o[:, qt, he:he + 1], scalar2=None,
                            op0=ALU.mult))
                        ps = ps_a if pg == 0 else ps_b
                        p.wait("tensor", eV)
                        p.wait("tensor", eA)
                        T(lambda t, pg=pg, ps=ps: t.transpose(
                            out=ps[:, 0:128], in_=tmpg[:, pg, :], identity=id_sb[:]))
                        p.wait("vector", eT)
                        V(lambda v, e=e, qtl=qtl, ps=ps: v.tensor_copy(
                            out=w_he[:, e, qtl * 128:(qtl + 1) * 128], in_=ps[:, 0:128]))
                for dc in range(8):
                    par = dc % 2
                    p.wait("sync", eT)
                    tk = dma("sync", dF, ow_sl[:, par, :, :],
                             ow_d[h * 8:(h + 1) * 8, :, dc * 128:(dc + 1) * 128]
                             .rearrange("e p u -> p e u"))
                    p.wait_tok("tensor", *tk)
                    p.wait("tensor", eV)
                    ps = ps_a if par == 0 else ps_b
                    p.wait("tensor", eA)
                    for e in range(8):
                        T(lambda t, e=e, par=par, ps=ps: t.matmul(
                            out=ps[:, :], lhsT=ow_sl[:, par, e, :],
                            rhs=w_he[:, e, :],
                            start=(e == 0), stop=(e == 7)), inc=(1 if e == 7 else 0))
                    p.wait("scalar", eT)
                    pr2 = (tc * 8 + dc) % 2
                    if owTok[pr2] is not None:
                        p.wait_tok("scalar", owTok[pr2])
                    tae = Ac(lambda a, pr2=pr2, ps=ps: a.copy(
                        out=evac[:, pr2, :], in_=ps[:, :]))
                    p.wait_tok("gpsimd", *tae)
                    tkg = dma("gpsimd", dG, oscr_d[dc, :, tc * 512:(tc + 1) * 512],
                              evac[:, pr2, :], accum=True)
                    owTok[pr2] = tkg[0]
        if s == 0 and "scrF" in tap_d:
            join()
            tap("scrF", oscr_d.ap())

    def st_final(s):
        join()
        for tt in range(TT):
            par = tt % 2
            if tt >= 2:
                p.wait("sync", eT)
                p.wait("sync", eV)
            tk1 = dma("sync", dR, fin_in[:, par, :, :],
                      oscr_d[:, :, tt * 128:(tt + 1) * 128].rearrange("c p u -> p c u"))
            tk2 = dma("sync", dX, fin_x[:, par, :], x_d[s, tt * 128:(tt + 1) * 128, :])
            p.wait_tok("tensor", *tk1)
            if tt >= 1:
                p.wait("tensor", eV)
            for c in range(8):
                T(lambda t, par=par, c=c: t.transpose(
                    out=ps_bigf[:, c * 128:(c + 1) * 128],
                    in_=fin_in[:, par, c, :], identity=id_sb[:]),
                  inc=(1 if c == 7 else 0))
            p.wait_tok("vector", *tk2)
            p.wait("vector", eT)
            if tt >= 2:
                p.wait("vector", dO)
            tv = V(lambda v, par=par: v.tensor_tensor(
                out=fin_o[:, par, :], in0=ps_bigf, in1=fin_x[:, par, :], op=ALU.add))
            p.wait_tok("sync", *tv)
            dma("sync", dO, out_d[s, tt * 128:(tt + 1) * 128, :], fin_o[:, par, :])

    stages = ["ln", "gates", "attn", "ffn", "final"]
    upto_i = stages.index(upto)
    nseq_eff = NSEQ if upto_i >= 4 else 1
    for s in range(nseq_eff):
        st_ln(s)
        if upto_i >= 1:
            st_gates(s)
        if upto_i >= 2:
            st_attn(s)
        if upto_i >= 3:
            st_ffn(s)
        if upto_i >= 4:
            st_final(s)

    p.wait("sync", dO)
    p.wait("sync", dG)
    p.wait("gpsimd", dO)

    with contextlib.ExitStack() as es:
        semh = {name: es.enter_context(nc.semaphore(name)) for name in p.cnt}
        block = es.enter_context(nc.Block())

        def runner(items):
            def body(eng):
                for kind, a, b in items:
                    if kind == "wait":
                        eng.wait_ge(semh[a], b)
                    else:
                        ins = a(eng)
                        for sm, amt in b:
                            ins = ins.then_inc(semh[sm], amt)
            return body

        for ename in ENGS:
            getattr(block, ename)(runner(p.items[ename]))
    return nc, tap_d, sbuf_left


def kernel(L0_attn_x, L0_attn_w_qkv, L0_attn_w_out, L0_ffn_q_w, L0_ffn_k_w,
           L0_ffn_v_w, L0_ffn_o_w, L0_ffn_sel_v_w, L0_ffn_sel_o_w,
           L0_ffn_route_scale, _taps=(), _upto="final"):
    consts = host_consts()
    nc, tap_d, _ = build(taps=_taps, upto=_upto)
    in_maps = []
    for c in range(NCORES):
        m = {
            "x": np.ascontiguousarray(np.asarray(L0_attn_x)[c * NSEQ:(c + 1) * NSEQ]),
            "w_qkv": np.asarray(L0_attn_w_qkv), "w_out": np.asarray(L0_attn_w_out),
            "q_w": np.asarray(L0_ffn_q_w), "k_w": np.asarray(L0_ffn_k_w),
            "v_w": np.asarray(L0_ffn_v_w), "o_w": np.asarray(L0_ffn_o_w),
            "sel_v": np.asarray(L0_ffn_sel_v_w), "sel_o": np.asarray(L0_ffn_sel_o_w),
            "route_scale": np.asarray(L0_ffn_route_scale),
        }
        m.update(consts)
        in_maps.append(m)
    res = run_bass_kernel_spmd(nc, in_maps, core_ids=list(range(NCORES)))
    out = np.concatenate([r["out"] for r in res.results], axis=0)
    if _taps:
        return out, {t: res.results[0]["tap_" + t] for t in _taps}
    return out
